# version 12
# speedup vs baseline: 1.5887x; 1.0051x over previous
import numpy as np
import concourse.bass as bass
import concourse.mybir as mybir
from concourse.alu_op_type import AluOpType as A

F32 = mybir.dt.float32
BF16 = mybir.dt.bfloat16
AF = mybir.ActivationFunctionType
AX = mybir.AxisListType

EPS64 = float(np.finfo(np.float64).eps)
LN1_EPS = 1e-5
LN3_EPS = 1e-6
QCLIP = 128.0 - 1e-6
Q_B = 128.0
SH_SWAP = [i + 1 - 2 * (i % 2) for i in range(32)]


class Cfg:
    def __init__(self, S=2048, E=2048, C=512, F=8192):
        self.S, self.E, self.C, self.F = S, E, C, F
        self.ST = S // 128
        self.EC = E // 128
        self.CC = C // 128
        self.FC = F // 128
        self.FS = F // 512
        self.SSL = S // 512
        self.ESL = E // 512
        self.SCALE_INV = 1.0 / 12.0
        assert E == 4 * C and S % 512 == 0 and E % 512 == 0 and F % 2048 == 0


Y, G, T, V, SC = "sync", "gpsimd", "tensor", "vector", "scalar"


class KB:
    ENGS = (Y, G, T, V, SC)
    DMA_SEM = {Y: "dh", G: "dg", SC: "ds"}

    def __init__(self, nc):
        self.nc = nc
        self.ops = {e: [] for e in self.ENGS}
        self.ctr = {}
        self.sems = {}

    def sem(self, name):
        if name not in self.sems:
            self.sems[name] = self.nc.alloc_semaphore(name)
            self.ctr[name] = 0
        return self.sems[name]

    def emit(self, eng, fn):
        self.ops[eng].append(fn)

    def wait(self, eng, sem_name, val):
        if val is None or val <= 0:
            return
        s = self.sem(sem_name)
        self.emit(eng, lambda e, s=s, v=val: e.wait_ge(s, v))

    def op(self, eng, fn, inc=None, inc_by=1):
        if inc is None:
            self.emit(eng, fn)
            return None
        s = self.sem(inc)
        self.ctr[inc] += inc_by
        self.emit(eng, lambda e, s=s, n=inc_by: fn(e).then_inc(s, n))
        return self.ctr[inc]

    def dma(self, eng, out, in_, inc=None):
        if inc is None:
            inc = self.DMA_SEM[eng]
        return self.op(eng, lambda e: e.dma_start(out, in_), inc=inc, inc_by=16)

    def dmat(self, eng, out, in_, inc=None):
        if inc is None:
            inc = self.DMA_SEM[eng]
        return self.op(eng, lambda e: e.dma_start(out, in_, transpose=True),
                       inc=inc, inc_by=16)

    def drain(self, eng):
        self.emit(eng, lambda e: e.drain())

    EXTRA_DMA = {"sync": ("dh", "st", "tst", "tb", "wk0", "wk1", "wo0", "wo1",
                          "fc0", "fc1", "pj0", "pj1", "wd", "hf", "xl",
                          "pq0", "pq1", "pq2", "pq3", "pq4", "pq5", "pq6",
                          "pq7", "hq0", "hq1", "hq2", "hq3", "hq4", "hq5",
                          "hq6", "hq7", "hs0", "hs1", "hs2", "hs3",
                          "cbs0", "cbs1"),
                 "gpsimd": ("dg", "uv0", "uv1", "cbl0", "cbl1"),
                 "scalar": ("ds", "ds0", "ds1")}

    def barrier(self):
        for eng, extras in self.EXTRA_DMA.items():
            for dsem in extras:
                if dsem in self.ctr:
                    self.wait(eng, dsem, self.ctr[dsem])
        vals = {}
        for eng in self.ENGS:
            bs = "bar_" + eng
            s = self.sem(bs)
            self.ctr[bs] += 1
            vals[bs] = self.ctr[bs]
            self.emit(eng, lambda e, s=s: e.sem_inc(s, 1))
        for eng in self.ENGS:
            for bs, v in vals.items():
                if bs != "bar_" + eng:
                    self.wait(eng, bs, v)

    def finalize(self):
        with self.nc.Block() as block:
            deco = {Y: block.sync, G: block.gpsimd, T: block.tensor,
                    V: block.vector, SC: block.scalar}
            for eng in self.ENGS:
                items = list(self.ops[eng])
                if not items:
                    continue
                def make(items):
                    def body(e):
                        for f in items:
                            f(e)
                    return body
                deco[eng](make(items))


def ap(t, off, dims):
    return bass.AP(t, off, dims)


def build_nc(cfg: Cfg, phases=99):
    S, E, C, F = cfg.S, cfg.E, cfg.C, cfg.F
    ST, EC, CC, FC, FS = cfg.ST, cfg.EC, cfg.CC, cfg.FC, cfg.FS
    SSL, ESL = cfg.SSL, cfg.ESL
    SCI = cfg.SCALE_INV
    nc = bass.Bass("TRN2", target_bir_lowering=False)

    x_in = nc.dram_tensor("x", [S, E], F32, kind="ExternalInput")
    wdq = nc.dram_tensor("wdq", [C, E], F32, kind="ExternalInput")
    wdkv = nc.dram_tensor("wdkv", [C, E], F32, kind="ExternalInput")
    wuk = nc.dram_tensor("wuk", [E, C], F32, kind="ExternalInput")
    wuv = nc.dram_tensor("wuv", [E, C], F32, kind="ExternalInput")
    wuq = nc.dram_tensor("wuq", [E, C], F32, kind="ExternalInput")
    wkr = nc.dram_tensor("wkr", [E, E], F32, kind="ExternalInput")
    wqr = nc.dram_tensor("wqr", [E, C], F32, kind="ExternalInput")
    wo = nc.dram_tensor("wo", [E, E], F32, kind="ExternalInput")
    wfc = nc.dram_tensor("wfc", [F, E], F32, kind="ExternalInput")
    wproj = nc.dram_tensor("wproj", [E, F], F32, kind="ExternalInput")
    g1b = nc.dram_tensor("g1b", [128, C], F32, kind="ExternalInput")
    g2b = nc.dram_tensor("g2b", [128, C], F32, kind="ExternalInput")
    g3b = nc.dram_tensor("g3b", [128, E], F32, kind="ExternalInput")
    c2t = nc.dram_tensor("c2t", [128, S], F32, kind="ExternalInput")
    s2t = nc.dram_tensor("s2t", [128, S], F32, kind="ExternalInput")
    ident_in = nc.dram_tensor("identw", [128, 128], BF16, kind="ExternalInput")
    ones_r = nc.dram_tensor("ones_r", [1, 128], F32, kind="ExternalInput")
    y_out = nc.dram_tensor("y", [S, E], F32, kind="ExternalOutput")

    h_scr = nc.dram_tensor("h_scr", [EC, S, 128], BF16)
    cqr_scr = nc.dram_tensor("cqr_scr", [CC, S, 128], BF16)
    cq_scr = nc.dram_tensor("cq_scr", [CC, S, 128], BF16)
    ckv_scr = nc.dram_tensor("ckv_scr", [CC, S, 128], BF16)
    wdqT_scr = nc.dram_tensor("wdqT_scr", [EC, C, 128], BF16)
    wdkvT_scr = nc.dram_tensor("wdkvT_scr", [EC, C, 128], BF16)
    wqrT_scr = nc.dram_tensor("wqrT_scr", [CC, E, 128], BF16)
    wkrT_scr = nc.dram_tensor("wkrT_scr", [EC, E, 128], BF16)
    woT_scr = nc.dram_tensor("woT_scr", [EC, E, 128], BF16)
    tfc_scr = nc.dram_tensor("tfc_scr", [EC, F, 128], BF16)
    tpj_scr = nc.dram_tensor("tpj_scr", [FC, E, 128], BF16)
    xq_scr = nc.dram_tensor("xq_scr", [EC, S, 128], BF16)
    x2_scr = nc.dram_tensor("x2_scr", [S, E], F32)
    h2_scr = nc.dram_tensor("h2_scr", [S, F], BF16)
    xh_scr = nc.dram_tensor("xh_scr", [FC, S, 128], BF16)

    sb = nc.alloc_sbuf_tensor
    BUF_A = sb("BUF_A", [128, EC * S], BF16, is_transpose_dest=True)
    BUF_B = sb("BUF_B", [128, EC * S], BF16, is_transpose_dest=True)
    WS = sb("WS", [128, max(EC * 512, CC * S, EC * C)], BF16,
            is_transpose_dest=True)
    MIX = sb("MIX", [128, 9728], BF16, is_transpose_dest=True)
    FMIX = sb("FMIX", [128, 9472], F32)
    wf32 = sb("wf32", [128, 512], F32)
    absd = sb("absd", [128, 512], F32)

    acc_fc = sb("acc_fc", [128, max(FC * (E // 512), EC * (F // 512)) + 2], F32)
    cols = sb("cols", [128, 64], F32)
    ccol = sb("ccol", [128, ST], F32)
    cpcol = sb("cpcol", [128, ST], F32)
    gscol = sb("gscol", [128, ST], F32)
    gpcol = sb("gpcol", [128, ST], F32)
    gamc = sb("gamc", [128, 4], F32)
    gam2 = sb("gam2", [128, 4], F32)
    identb = sb("identb", [128, 128], BF16)
    onesr = sb("onesr", [1, 128], F32)


    o_x = 0
    o_x2 = 2048
    o_u = 4096
    o_g3 = 6144
    o_tsl = 8192
    o_rp = 8704
    o_g1 = 8192
    o_g2 = 8704
    NST = max(FS * ST, 1)
    o_ssum, o_ssq, o_smx, o_smn = 0, NST, 2 * NST, 3 * NST

    m_exp = 0
    m_xqt = 0
    m_qr = 2048
    m_o = 2048 + EC * 256
    m_tt = m_o + CC * 128
    m_at = m_tt + 1024
    assert m_at + ST * 128 <= 9728, m_at
    m_hbt = 0
    m_mq = 4096
    m_wuv = 4096 + CC * C
    m_pjd = 2048
    m_h2a = 0

    p_sc = nc.alloc_psum_tensor("p_sc", [128, 4 * 512], F32)
    p_a = nc.alloc_psum_tensor("p_a", [128, 512], F32)
    p_b = nc.alloc_psum_tensor("p_b", [128, 512], F32)
    p_c = nc.alloc_psum_tensor("p_c", [128, 512], F32)
    p_t = nc.alloc_psum_tensor("p_t", [128, 1024], BF16)

    k = KB(nc)

    def col(i):
        return cols[:, i:i + 1]

    def blocked_store(eng, scr, Rtot, tens, coff, rc, nchunks):
        dims = [[128, 128], [128 * Rtot, nchunks], [1, 128]]
        return k.dma(eng, ap(scr, rc * 128 * 128, dims),
                     tens[:, coff:coff + nchunks * 128], inc="st")

    def rsqrt_col(src_i, dst_i, tmp_i):
        vv = k.op(V, lambda e: e.reciprocal(col(tmp_i), col(src_i)), inc="ev")
        k.wait(SC, "ev", vv)
        sv = k.op(SC, lambda e: e.activation(col(dst_i), col(tmp_i), AF.Sqrt), inc="es")
        k.wait(V, "es", sv)

    def rsqrt_col_lnexp(src_i, dst_i, tmp_i):
        vv = k.op(V, lambda e: e.drain(), inc="ev")
        k.wait(SC, "ev", vv)
        k.op(SC, lambda e: e.activation(col(tmp_i), col(src_i), AF.Ln))
        k.drain(SC)
        sv = k.op(SC, lambda e: e.activation(col(dst_i), col(tmp_i), AF.Exp,
                                             scale=-0.5), inc="es")
        k.wait(V, "es", sv)

    k.dma(Y, identb[:, :], ident_in[:, :], inc="tb")
    k.dma(Y, onesr[:, :], ones_r[:, :], inc="tb")

    def cast_and_block(mat, scr, R, W, boffs=(0, 4096)):
        RC, WC = R // 128, W // 128
        last = None
        for rc in range(RC):
            par = rc % 2
            boff = m_hbt + boffs[par]
            ls, ss = "cbl%d" % par, "cbs%d" % par
            if k.ctr.get(ss, 0) > 0:
                k.wait(G, ss, k.ctr[ss])
            lv = k.dma(G, MIX[:, boff:boff + W], mat[rc * 128:(rc + 1) * 128, :],
                       inc=ls)
            k.wait(Y, ls, lv)
            dims = [[128, 128], [128 * R, WC], [1, 128]]
            last = k.dma(Y, ap(scr, rc * 128 * 128, dims), MIX[:, boff:boff + WC * 128],
                         inc=ss)
        return last

    cast_and_block(wdq, wdqT_scr, C, E)
    cast_and_block(wdkv, wdkvT_scr, C, E)
    cast_and_block(wqr, wqrT_scr, E, C)
    cast_and_block(wkr, wkrT_scr, E, E)
    cast_and_block(wo, woT_scr, E, E)
    k.barrier()

    for qt in range(ST):
        xoff = o_x if qt % 2 == 0 else o_x2
        if qt >= 2:
            k.wait(Y, "es", k.ctr["es"])
        ld = k.dma(Y, FMIX[:, xoff:xoff + E], x_in[qt * 128:(qt + 1) * 128, :],
                   inc="xl")
        k.wait(V, "xl", ld)
        k.op(V, lambda e, xo=xoff: e.reduce_sum(col(0), FMIX[:, xo:xo + E], axis=AX.X))
        k.drain(V)
        k.op(V, lambda e, xo=xoff: e.scalar_tensor_tensor(
            FMIX[:, o_u:o_u + E], FMIX[:, xo:xo + E], 1.0, FMIX[:, xo:xo + E],
            A.mult, A.mult, accum_out=col(1)))
        k.drain(V)
        k.op(V, lambda e: e.tensor_scalar(col(2), col(0), 1.0 / E, None, A.mult))
        k.drain(V)
        k.op(V, lambda e: e.scalar_tensor_tensor(col(3), col(2), 1.0, col(2),
                                                 A.mult, A.mult))
        k.op(V, lambda e: e.tensor_scalar(col(4), col(1), 1.0 / E, None, A.mult))
        k.drain(V)
        k.op(V, lambda e: e.tensor_tensor(col(5), col(4), col(3), A.subtract))
        k.drain(V)
        k.op(V, lambda e: e.tensor_scalar(col(5), col(5), 1.0, LN1_EPS, A.mult, A.add))
        k.drain(V)
        rsqrt_col(5, 6, 7)
        vv = k.op(V, lambda e: e.tensor_scalar(col(8), col(2), -1.0, col(6),
                                               A.mult, A.mult), inc="ev")
        k.wait(SC, "ev", vv)
        hb = m_hbt + (qt % 2) * E
        if qt >= 2:
            k.wait(SC, "st", k.ctr["st"])
        sv = k.op(SC, lambda e, xo=xoff, hb=hb: e.activation(
            MIX[:, hb:hb + E], FMIX[:, xo:xo + E], AF.Identity,
            bias=col(8), scale=col(6)), inc="es")
        k.wait(Y, "es", sv)
        blocked_store(Y, h_scr, S, MIX, hb, qt, EC)
    k.barrier()

    if phases < 2:
        k.barrier()
        k.finalize()
        return nc

    for ec in range(EC):
        k.dma(G, BUF_A[:, ec * C:(ec + 1) * C], wuq[ec * 128:(ec + 1) * 128, :])
        lw = k.dma(G, BUF_B[:, ec * C:(ec + 1) * C], wuk[ec * 128:(ec + 1) * 128, :])
    k.wait(T, "dg", lw)
    for cc in range(CC):
        pa = p_a if cc % 2 == 0 else p_b
        def mm_mq(e, cc=cc, pa=pa):
            for ec in range(EC):
                last = e.matmul(pa[:, 0:C],
                         BUF_A[:, ec * C + cc * 128:ec * C + (cc + 1) * 128],
                         BUF_B[:, ec * C:(ec + 1) * C],
                         start=(ec == 0), stop=(ec == EC - 1))
            return last
        tv = k.op(T, mm_mq, inc="et")
        k.wait(V, "et", tv)
        vv = k.op(V, lambda e, cc=cc, pa=pa: e.tensor_copy(
            MIX[:, m_mq + cc * C:m_mq + (cc + 1) * C], pa[:, 0:C]), inc="ev")
        k.wait(T, "ev", vv)
    k.barrier()

    if phases < 3:
        k.barrier()
        k.finalize()
        return nc

    h_fm = BUF_A
    for ec in range(EC):
        k.dmat(Y, h_fm[:, ec * S:(ec + 1) * S], h_scr[ec, :, :], inc="hf")

    if phases < 4:
        k.barrier()
        k.finalize()
        return nc

    k.dma(Y, FMIX[:, o_g1:o_g1 + C], g1b[:, :], inc="tb")
    k.dma(Y, FMIX[:, o_g2:o_g2 + C], g2b[:, :], inc="tb")
    k.wait(T, "hf", k.ctr["hf"])

    def rms_tm(psrc, sb_src, gbc, out_ap, width, cb):
        sv = k.op(SC, lambda e: e.activation(
            MIX[:, m_wuv:m_wuv + width], sb_src, AF.Square,
            accum_out=col(cb)), inc="es")
        k.wait(V, "es", sv)
        k.op(V, lambda e: e.tensor_scalar(col(cb + 1), col(cb), 1.0 / width,
                                          EPS64, A.mult, A.add))
        k.drain(V)
        rsqrt_col(cb + 1, cb + 2, cb + 3)
        return k.op(V, lambda e: e.scalar_tensor_tensor(
            out_ap, psrc, col(cb + 2), gbc, A.mult, A.mult), inc="ev")

    ckvtm_o = 3 * CC * S
    k.wait(V, "tb", k.ctr["tb"])
    for passi, (wscr, rscr, rawscr, goff) in enumerate(
            ((wdqT_scr, cq_scr, cqr_scr, o_g1), (wdkvT_scr, ckv_scr, None, o_g2))):
        for ec in range(EC):
            wl = k.dmat(Y, WS[:, ec * C:(ec + 1) * C], wscr[ec, :, :], inc="wd")
        k.wait(T, "wd", k.ctr["wd"])
        for qt in range(ST):
            pa = p_a if qt % 2 == 0 else p_b
            def mm_cq(e, qt=qt, pa=pa):
                for ec in range(EC):
                    last = e.matmul(pa[:, 0:C],
                             h_fm[:, ec * S + qt * 128:ec * S + (qt + 1) * 128],
                             WS[:, ec * C:(ec + 1) * C],
                             start=(ec == 0), stop=(ec == EC - 1))
                return last
            tv = k.op(T, mm_cq, inc="et")
            k.wait(V, "et", tv)
            k.wait(SC, "et", tv)
            k.wait(V, "st", k.ctr["st"])
            vv = k.op(V, lambda e, pa=pa: e.tensor_copy(MIX[:, 0:C],
                                                        pa[:, 0:C]), inc="ev")
            k.wait(SC, "ev", vv)
            if rawscr is not None:
                k.wait(Y, "ev", vv)
                blocked_store(Y, rawscr, S, MIX, 0, qt, CC)
            vv = rms_tm(pa[:, 0:C], MIX[:, 0:C], FMIX[:, goff:goff + C],
                        MIX[:, C:2 * C], C, 10)
            k.wait(Y, "ev", vv)
            blocked_store(Y, rscr, S, MIX, C, qt, CC)
            k.wait(T, "ev", vv)
    k.barrier()

    if phases < 5:
        k.barrier()
        k.finalize()
        return nc

    k.dma(Y, FMIX[:, 0:S], c2t[:, :], inc="tb")
    tl = k.dma(Y, FMIX[:, S:2 * S], s2t[:, :], inc="tb")
    k.wait(V, "tb", tl)
    kr_fm = BUF_B
    RW = 256
    kr_loads = []
    for eoc in range(EC):
        base = (eoc % 2) * 2048
        if eoc >= 2:
            k.wait(Y, "et", kr_loads[eoc - 2])
        for eic in range(EC):
            wl = k.dmat(Y, WS[:, base + eic * 128:base + (eic + 1) * 128],
                        wkrT_scr[eic, eoc * 128:(eoc + 1) * 128, :],
                        inc="wk%d" % (eoc % 2))
        k.wait(T, "wk%d" % (eoc % 2), wl)
        tv_last = None
        for sl in range(S // RW):
            pa = p_a if sl % 2 == 0 else p_b
            def mm_kr(e, eoc=eoc, sl=sl, base=base, pa=pa):
                for eic in range(EC):
                    last = e.matmul(pa[:, 0:RW],
                             WS[:, base + eic * 128:base + (eic + 1) * 128],
                             h_fm[:, eic * S + sl * RW:eic * S + (sl + 1) * RW],
                             start=(eic == 0), stop=(eic == EC - 1))
                return last
            tv_last = k.op(T, mm_kr, inc="et")
            k.wait(V, "et", tv_last)
            vv = k.op(V, lambda e, pa=pa: e.tensor_copy(FMIX[:, o_rp:o_rp + RW],
                                                        pa[:, 0:RW]), inc="ev")
            k.wait(T, "ev", vv)
            k.drain(V)
            k.op(V, lambda e: e.stream_shuffle(FMIX[:, o_rp + RW:o_rp + 2 * RW],
                                               FMIX[:, o_rp:o_rp + RW], SH_SWAP))
            k.op(V, lambda e, sl=sl: e.tensor_tensor(
                FMIX[:, o_rp + 2 * RW:o_rp + 3 * RW], FMIX[:, o_rp:o_rp + RW],
                FMIX[:, sl * RW:(sl + 1) * RW], A.mult))
            k.drain(V)
            k.op(V, lambda e, sl=sl: e.tensor_tensor(
                FMIX[:, o_rp:o_rp + RW],
                FMIX[:, o_rp + RW:o_rp + 2 * RW],
                FMIX[:, S + sl * RW:S + (sl + 1) * RW], A.mult))
            k.drain(V)
            k.op(V, lambda e, eoc=eoc, sl=sl: e.tensor_tensor(
                kr_fm[:, eoc * S + sl * RW:eoc * S + (sl + 1) * RW],
                FMIX[:, o_rp + 2 * RW:o_rp + 3 * RW],
                FMIX[:, o_rp:o_rp + RW], A.add))
            k.drain(V)
        kr_loads.append(tv_last)
    k.barrier()

    if phases < 6:
        k.barrier()
        k.finalize()
        return nc

    k.dma(Y, FMIX[:, o_g3:o_g3 + E], g3b[:, :], inc="tb")
    ckv_fm_o, cq_fm_o, aq_o = 0, CC * S, 2 * CC * S
    for cc in range(CC):
        k.dmat(Y, BUF_A[:, ckv_fm_o + cc * S:ckv_fm_o + (cc + 1) * S],
               ckv_scr[cc, :, :], inc="hf")
        k.dmat(Y, BUF_A[:, cq_fm_o + cc * S:cq_fm_o + (cc + 1) * S],
               cq_scr[cc, :, :], inc="hf")
        k.dmat(Y, WS[:, cc * S:(cc + 1) * S], cqr_scr[cc, :, :], inc="hf")
    for qt in range(ST):
        dims = [[128, 128], [S * 128, CC], [1, 128]]
        k.dma(Y, BUF_A[:, ckvtm_o + qt * C:ckvtm_o + (qt + 1) * C],
              ap(ckv_scr, qt * 128 * 128, dims), inc="hf")
    k.wait(T, "hf", k.ctr["hf"])
    for cc2 in range(CC):
        for sl in range(S // 512):
            pa = p_a if sl % 2 == 0 else p_b
            def mm_aq(e, cc2=cc2, sl=sl, pa=pa):
                for cc1 in range(CC):
                    last = e.matmul(pa[:, 0:512],
                             MIX[:, m_mq + cc1 * C + cc2 * 128:
                                 m_mq + cc1 * C + (cc2 + 1) * 128],
                             WS[:, cc1 * S + sl * 512:cc1 * S + (sl + 1) * 512],
                             start=(cc1 == 0), stop=(cc1 == CC - 1))
                return last
            tv = k.op(T, mm_aq, inc="et")
            k.wait(V, "et", tv)
            vv = k.op(V, lambda e, cc2=cc2, sl=sl, pa=pa: e.tensor_copy(
                BUF_A[:, aq_o + cc2 * S + sl * 512:aq_o + cc2 * S + (sl + 1) * 512],
                pa[:, 0:512]), inc="ev")
            k.wait(T, "ev", vv)
    k.barrier()

    ao_o = 0
    for e2s in range(ESL):
        for e1c in range(EC):
            wvo = m_wuv + (e1c % 2) * 512
            woo = m_wuv + 1024 + (e1c % 2) * 512
            if e2s > 0 or e1c >= 2:
                k.wait(G, "et", k.ctr["et"])
                k.wait(Y, "et", k.ctr["et"])
            lv = k.dma(G, MIX[:, wvo:wvo + C], wuv[e1c * 128:(e1c + 1) * 128, :],
                       inc="uv%d" % (e1c % 2))
            wl = k.dmat(Y, MIX[:, woo:woo + 512],
                        woT_scr[e1c, e2s * 512:(e2s + 1) * 512, :],
                        inc="wo%d" % (e1c % 2))
            k.wait(T, "wo%d" % (e1c % 2), wl)
            k.wait(T, "uv%d" % (e1c % 2), lv)
            def mm_ao(e, e1c=e1c, wvo=wvo, woo=woo):
                for cc in range(CC):
                    last = e.matmul(p_sc[:, cc * 512:(cc + 1) * 512],
                             MIX[:, wvo + cc * 128:wvo + (cc + 1) * 128],
                             MIX[:, woo:woo + 512],
                             start=(e1c == 0), stop=(e1c == EC - 1))
                return last
            tv = k.op(T, mm_ao, inc="et")
        k.wait(V, "et", tv)
        for cc in range(CC):
            vv = k.op(V, lambda e, cc=cc, e2s=e2s: e.tensor_copy(
                WS[:, ao_o + cc * E + e2s * 512:ao_o + cc * E + (e2s + 1) * 512],
                p_sc[:, cc * 512:(cc + 1) * 512]), inc="ev")
        k.wait(T, "ev", vv)
    k.barrier()

    if phases < 7:
        k.barrier()
        k.finalize()
        return nc

    QG = 2

    def qr_group(g):
        q0 = g * QG * 128
        W = QG * 128
        if g > 0:
            k.wait(Y, "st", k.ctr["st"])
            k.wait(Y, "ev", k.ctr["ev"])
        k.dma(Y, FMIX[:, o_tsl:o_tsl + W], c2t[:, q0:q0 + W], inc="tb")
        tv1 = k.dma(Y, FMIX[:, o_tsl + W:o_tsl + 2 * W], s2t[:, q0:q0 + W], inc="tb")
        k.wait(V, "tb", tv1)
        for eoc in range(EC):
            pa = p_a if eoc % 2 == 0 else p_b
            qb = (eoc % 2) * (CC * 128)
            if eoc >= 2:
                k.wait(Y, "et", k.ctr["et"] - 1)
            for cc in range(CC):
                wl = k.dmat(Y, MIX[:, m_exp + qb + cc * 128:m_exp + qb + (cc + 1) * 128],
                            wqrT_scr[cc, eoc * 128:(eoc + 1) * 128, :],
                            inc="wk%d" % (eoc % 2))
            k.wait(T, "wk%d" % (eoc % 2), wl)
            def mm_qr(e, q0=q0, W=W, pa=pa, qb=qb):
                for cc in range(CC):
                    last = e.matmul(pa[:, 0:W],
                             MIX[:, m_exp + qb + cc * 128:m_exp + qb + (cc + 1) * 128],
                             BUF_A[:, cq_fm_o + cc * S + q0:cq_fm_o + cc * S + q0 + W],
                             start=(cc == 0), stop=(cc == CC - 1))
                return last
            tv = k.op(T, mm_qr, inc="et")
            k.wait(V, "et", tv)
            vv = k.op(V, lambda e, pa=pa, W=W: e.tensor_copy(
                FMIX[:, o_rp:o_rp + W], pa[:, 0:W]), inc="ev")
            k.wait(T, "ev", vv)
            k.drain(V)
            k.op(V, lambda e, W=W: e.stream_shuffle(
                FMIX[:, o_rp + 256:o_rp + 256 + W], FMIX[:, o_rp:o_rp + W], SH_SWAP))
            k.op(V, lambda e, W=W: e.tensor_tensor(
                FMIX[:, o_rp + 512:o_rp + 512 + W], FMIX[:, o_rp:o_rp + W],
                FMIX[:, o_tsl:o_tsl + W], A.mult))
            k.drain(V)
            k.op(V, lambda e, W=W: e.tensor_tensor(
                FMIX[:, o_rp:o_rp + W],
                FMIX[:, o_rp + 256:o_rp + 256 + W],
                FMIX[:, o_tsl + W:o_tsl + 2 * W], A.mult))
            k.drain(V)
            k.op(V, lambda e, eoc=eoc, W=W: e.tensor_tensor(
                MIX[:, m_qr + eoc * W:m_qr + (eoc + 1) * W],
                FMIX[:, o_rp + 512:o_rp + 512 + W],
                FMIX[:, o_rp:o_rp + W], A.add), inc="ev")
            k.drain(V)

    prep_descs = []

    def gamma_total(acc_n, thr_i, nthr_i, neg_i):
        def go():
            k.wait(V, "es", k.ctr["es"])
            k.op(V, lambda e: e.reduce_sum(col(30), acc_fc[:, 0:acc_n], axis=AX.X))
            k.op(V, lambda e: e.memset(col(31), 1.0))
            k.drain(V)
            vv = k.op(V, lambda e: e.drain(), inc="ev")
            k.wait(T, "ev", vv)
            tv = k.op(T, lambda e: e.matmul(p_c[0:1, 0:1], col(30), col(31),
                                            start=True, stop=True), inc="et")
            k.wait(V, "et", tv)
            k.op(V, lambda e: e.tensor_scalar(cols[0:1, 32:33], p_c[0:1, 0:1],
                                              1.0 / (F * E), None, A.mult))
            k.drain(V)
            vv = k.op(V, lambda e: e.drain(), inc="ev")
            k.wait(T, "ev", vv)
            tv = k.op(T, lambda e: e.matmul(p_c[:, 1:2], onesr[:, :],
                                            cols[0:1, 32:33], start=True, stop=True),
                      inc="et")
            k.wait(V, "et", tv)
            k.op(V, lambda e: e.tensor_scalar(gamc[:, thr_i:thr_i + 1], p_c[:, 1:2],
                                              0.5, 0.5e-6, A.mult, A.add))
            k.op(V, lambda e: e.tensor_scalar(gam2[:, neg_i:neg_i + 1], p_c[:, 1:2],
                                              -1.0, None, A.mult))
            k.drain(V)
            k.op(V, lambda e: e.tensor_scalar(gamc[:, nthr_i:nthr_i + 1],
                                              gamc[:, thr_i:thr_i + 1], -1.0, None,
                                              A.mult))
            k.drain(V)
            vv = k.op(V, lambda e: e.drain(), inc="ev")
            k.wait(T, "ev", vv)
            k.wait(G, "ev", vv)
            k.wait(SC, "ev", vv)
        return go

    for i in range(FC):
        for hh in range(E // 512):
            prep_descs.append(("abs", wfc, i, hh, i * (E // 512) + hh))
    prep_descs.append(("total", FC * (E // 512), 0, 1, 0))
    for i in range(FC):
        for hh in range(E // 512):
            prep_descs.append(("tern", wfc, tfc_scr, F, i, hh, 0))
    FC_PREP_END = len(prep_descs)
    for i in range(EC):
        for q in range(F // 512):
            prep_descs.append(("abs", wproj, i, q, i * (F // 512) + q))
    prep_descs.append(("total", EC * (F // 512), 2, 3, 1))
    for i in range(EC):
        for q in range(F // 512):
            prep_descs.append(("tern", wproj, tpj_scr, E, i, q, 2))

    prep_state = {"done": 0, "issued": 0, "lv": {}, "cons": [0, 0], "st": 0,
                  "ttv": 0, "tcv": 0}

    def prep_issue_loads(depth=2):
        while (prep_state["issued"] < len(prep_descs)
               and prep_state["issued"] < prep_state["done"] + depth):
            n = prep_state["issued"]
            d = prep_descs[n]
            if d[0] == "total":
                if n > prep_state["done"]:
                    break
                prep_state["issued"] = n + 1
                continue
            if d[0] == "abs":
                _, wsrc, i, hh = d[0], d[1], d[2], d[3]
            else:
                wsrc, i, hh = d[1], d[4], d[5]
            par = n % 2
            if prep_state["cons"][par]:
                sem, val = prep_state["cons"][par]
                k.wait(SC, sem, val)
            dst = wf32[:, 0:512] if par == 0 else absd[:, 0:512]
            lv = k.dma(SC, dst,
                       wsrc[i * 128:(i + 1) * 128, hh * 512:(hh + 1) * 512],
                       inc="ds%d" % par)
            prep_state["lv"][n] = lv
            prep_state["issued"] = n + 1

    def run_prep(nu):
        while nu > 0 and prep_state["done"] < len(prep_descs):
            n = prep_state["done"]
            d = prep_descs[n]
            if d[0] == "total":
                gamma_total(d[1], d[2], d[3], d[4])()
                prep_state["done"] = n + 1
                nu -= 1
                continue
            prep_issue_loads()
            par = n % 2
            buf = wf32[:, 0:512] if par == 0 else absd[:, 0:512]
            lv = prep_state["lv"].pop(n)
            if d[0] == "abs":
                slot = d[4]
                k.wait(SC, "ds%d" % par, lv)
                if prep_state["ttv"]:
                    k.wait(SC, "et", prep_state["ttv"])
                if par == 1 and prep_state["st"]:
                    k.wait(SC, "tst", prep_state["st"])
                sv = k.op(SC, lambda e, slot=slot, buf=buf, par=par: e.activation(
                    MIX[:, m_tt + par * 512:m_tt + par * 512 + 512], buf, AF.Abs,
                    accum_out=acc_fc[:, slot:slot + 1]), inc="es")
                prep_state["cons"][par] = ("es", sv)
            else:
                _, wsrc, scr, Rtot, i, hh, gi = d
                k.wait(V, "ds%d" % par, lv)
                if prep_state["ttv"]:
                    k.wait(V, "et", prep_state["ttv"])
                k.op(V, lambda e, buf=buf, gi=gi: e.tensor_scalar(
                    p_c[:, 0:512], buf, gamc[:, gi:gi + 1], None, A.is_gt))
                k.drain(V)
                gv = k.op(V, lambda e, buf=buf, gi=gi: e.scalar_tensor_tensor(
                    MIX[:, m_tt:m_tt + 512], buf,
                    gamc[:, gi + 1:gi + 2], p_c[:, 0:512], A.is_lt, A.subtract),
                    inc="ev")
                prep_state["cons"][par] = ("ev", gv)
                k.wait(T, "ev", gv)
                if prep_state["tcv"]:
                    k.wait(T, "es", prep_state["tcv"])
                def mm_tt(e):
                    for u in range(4):
                        dc = 256 + u * 128 if u < 2 else 768 + (u - 2) * 128
                        last = e.transpose(p_t[:, dc:dc + 128],
                                           MIX[:, m_tt + u * 128:
                                               m_tt + (u + 1) * 128],
                                           identb[:, :])
                    return last
                tv = k.op(T, mm_tt, inc="et")
                prep_state["ttv"] = tv
                k.wait(SC, "et", tv)
                if prep_state["st"]:
                    k.wait(SC, "tst", prep_state["st"])
                cv = k.op(SC, lambda e: e.copy(
                    MIX[:, m_tt + 512:m_tt + 1024],
                    ap(p_t, 256, [[1024, 128], [512, 2], [1, 256]])), inc="es")
                prep_state["tcv"] = cv
                k.wait(Y, "es", cv)
                off = ((hh * 4) * (Rtot * 128) + (i // 4) * (512 * 128)
                       + (i % 4) * 128)
                prep_state["st"] = k.dma(
                    Y, ap(scr, off, [[512, 128], [128 * Rtot, 4], [1, 128]]),
                    MIX[:, m_tt + 512:m_tt + 1024], inc="tst")
            prep_state["done"] = n + 1
            nu -= 1
            prep_issue_loads()

    def run_prep_until(idx):
        while prep_state["done"] < idx:
            run_prep(1)

    def attn_tile(qt, qr_ev):
        k.wait(T, "ev", qr_ev)
        def mm_sc(e, qt=qt):
            for ks in range(SSL):
                for cc in range(CC):
                    last = e.matmul(p_sc[:, ks * 512:(ks + 1) * 512],
                             BUF_A[:, aq_o + cc * S + qt * 128:
                                   aq_o + cc * S + (qt + 1) * 128],
                             BUF_A[:, ckv_fm_o + cc * S + ks * 512:
                                   ckv_fm_o + cc * S + (ks + 1) * 512],
                             start=(cc == 0), stop=False)
                for ec in range(EC):
                    last = e.matmul(p_sc[:, ks * 512:(ks + 1) * 512],
                             MIX[:, m_qr + ec * (QG * 128) + (qt % QG) * 128:
                                 m_qr + ec * (QG * 128) + (qt % QG + 1) * 128],
                             kr_fm[:, ec * S + ks * 512:ec * S + (ks + 1) * 512],
                             start=False, stop=(ec == EC - 1))
            return last
        tv = k.op(T, mm_sc, inc="et")
        k.wait(V, "et", tv)
        k.wait(SC, "st", k.ctr["st"])
        k.op(V, lambda e: e.reduce_max(col(20), p_sc[:, 0:S], axis=AX.X))
        k.drain(V)
        vv = k.op(V, lambda e: e.tensor_scalar(col(21), col(20), -SCI, None, A.mult),
                  inc="ev")
        k.wait(SC, "ev", vv)
        sv = k.op(SC, lambda e: e.activation(
            MIX[:, m_exp:m_exp + S], p_sc[:, 0:S], AF.Exp,
            bias=col(21), scale=SCI, accum_out=col(22)), inc="es")
        k.wait(T, "es", sv)
        ev_evac = []
        for pair in range(ST // 2):
            po = 0 if pair % 2 == 0 else 512
            if pair >= 1:
                k.wait(T, "es", ev_evac[pair - 1])
            def mm_tr(e, pair=pair, po=po):
                last = e.transpose(p_t[:, po:po + 128],
                            MIX[:, m_exp + 2 * pair * 128:m_exp + (2 * pair + 1) * 128],
                            identb[:, :])
                return e.transpose(
                    p_t[:, po + 128:po + 256],
                    MIX[:, m_exp + (2 * pair + 1) * 128:m_exp + (2 * pair + 2) * 128],
                    identb[:, :])
            tvt = k.op(T, mm_tr, inc="et")
            k.wait(SC, "et", tvt)
            sv = k.op(SC, lambda e, pair=pair, po=po: e.copy(
                MIX[:, m_at + 2 * pair * 128:m_at + (2 * pair + 2) * 128],
                p_t[:, po:po + 256]), inc="es")
            ev_evac.append(sv)
            if pair >= 1:
                kt0 = (pair - 1) * 2
                def mm_o(e, kt0=kt0):
                    for kt in (kt0, kt0 + 1):
                        for cc in range(CC):
                            last = e.matmul(p_sc[:, cc * 512:cc * 512 + 128],
                                     BUF_A[:, ckvtm_o + kt * C + cc * 128:
                                           ckvtm_o + kt * C + (cc + 1) * 128],
                                     MIX[:, m_at + kt * 128:m_at + (kt + 1) * 128],
                                     start=(kt == 0), stop=(kt == ST - 1))
                    return last
                k.op(T, mm_o)
        k.wait(T, "es", ev_evac[-1])
        def mm_o_last(e):
            for kt in (ST - 2, ST - 1):
                for cc in range(CC):
                    last = e.matmul(p_sc[:, cc * 512:cc * 512 + 128],
                             BUF_A[:, ckvtm_o + kt * C + cc * 128:
                                   ckvtm_o + kt * C + (cc + 1) * 128],
                             MIX[:, m_at + kt * 128:m_at + (kt + 1) * 128],
                             start=(kt == 0), stop=(kt == ST - 1))
            return last
        tv = k.op(T, mm_o_last, inc="et")
        k.wait(V, "et", tv)
        for cc in range(CC):
            vv = k.op(V, lambda e, cc=cc: e.tensor_copy(
                MIX[:, m_o + cc * 128:m_o + (cc + 1) * 128],
                p_sc[:, cc * 512:cc * 512 + 128]), inc="ev")
        k.wait(T, "ev", vv)
        def mm_u(e):
            for es in range(ESL):
                for cc in range(CC):
                    last = e.matmul(p_sc[:, es * 512:(es + 1) * 512],
                             MIX[:, m_o + cc * 128:m_o + (cc + 1) * 128],
                             WS[:, ao_o + cc * E + es * 512:ao_o + cc * E + (es + 1) * 512],
                             start=(cc == 0), stop=(cc == CC - 1))
            return last
        tv = k.op(T, mm_u, inc="et")
        k.wait(V, "et", tv)
        k.wait(V, "st", k.ctr["st"])
        k.op(V, lambda e: e.tensor_copy(FMIX[:, o_u:o_u + E], p_sc[:, 0:E]))
        k.drain(V)
        vv = k.op(V, lambda e: e.scalar_tensor_tensor(
            FMIX[:, o_x2:o_x2 + E], FMIX[:, o_u:o_u + E], 1.0,
            FMIX[:, o_u:o_u + E], A.mult, A.mult, accum_out=col(23)), inc="ev")
        k.wait(T, "ev", vv)
        k.op(V, lambda e: e.reciprocal(col(24), col(22)))
        k.drain(V)
        k.op(V, lambda e: e.scalar_tensor_tensor(col(25), col(24), 1.0, col(24),
                                                 A.mult, A.mult))
        k.drain(V)
        k.op(V, lambda e: e.tensor_tensor(col(26), col(23), col(25), A.mult))
        k.drain(V)
        k.op(V, lambda e: e.tensor_scalar(col(26), col(26), 1.0 / E, EPS64,
                                          A.mult, A.add))
        k.drain(V)
        rsqrt_col_lnexp(26, 27, 28)
        k.op(V, lambda e: e.tensor_tensor(col(29), col(24), col(27), A.mult))
        k.drain(V)
        vv = k.op(V, lambda e: e.scalar_tensor_tensor(
            FMIX[:, o_x2:o_x2 + E], FMIX[:, o_u:o_u + E], col(29),
            FMIX[:, o_g3:o_g3 + E], A.mult, A.mult), inc="ev")
        k.wait(Y, "eg", k.ctr.get("eg", 0))
        xl = k.dma(Y, FMIX[:, o_x:o_x + E], x_in[qt * 128:(qt + 1) * 128, :],
                   inc="xl")
        k.wait(G, "xl", xl)
        k.wait(G, "ev", vv)
        gv = k.op(G, lambda e: e.tensor_tensor(FMIX[:, o_u:o_u + E],
                                               FMIX[:, o_x2:o_x2 + E],
                                               FMIX[:, o_x:o_x + E], A.add), inc="eg")
        k.wait(Y, "eg", gv)
        k.dma(Y, x2_scr[qt * 128:(qt + 1) * 128, :], FMIX[:, o_u:o_u + E], inc="st")
        k.wait(V, "eg", gv)
        k.op(V, lambda e: e.reduce_sum(col(40), FMIX[:, o_u:o_u + E], axis=AX.X))
        k.drain(V)
        k.op(V, lambda e: e.scalar_tensor_tensor(
            FMIX[:, o_x2:o_x2 + E], FMIX[:, o_u:o_u + E], 1.0,
            FMIX[:, o_u:o_u + E], A.mult, A.mult, accum_out=col(41)))
        k.drain(V)
        k.op(V, lambda e: e.reduce_max(col(42), FMIX[:, o_u:o_u + E], axis=AX.X))
        k.op(V, lambda e: e.tensor_reduce(col(43), FMIX[:, o_u:o_u + E],
                                          axis=AX.X, op=A.min))
        k.drain(V)
        k.op(V, lambda e: e.tensor_scalar(col(44), col(40), 1.0 / E, None, A.mult))
        k.drain(V)
        k.op(V, lambda e: e.scalar_tensor_tensor(col(45), col(44), 1.0, col(44),
                                                 A.mult, A.mult))
        k.op(V, lambda e: e.tensor_scalar(col(46), col(41), 1.0 / E, None, A.mult))
        k.drain(V)
        k.op(V, lambda e: e.tensor_tensor(col(46), col(46), col(45), A.subtract))
        k.drain(V)
        k.op(V, lambda e: e.tensor_scalar(col(47), col(46), 1.0, LN1_EPS,
                                          A.mult, A.add))
        k.drain(V)
        rsqrt_col_lnexp(47, 48, 49)
        k.op(V, lambda e: e.reciprocal(col(50), col(47)))
        k.drain(V)
        k.op(V, lambda e: e.tensor_tensor(col(51), col(46), col(50), A.mult))
        k.drain(V)
        k.op(V, lambda e: e.tensor_scalar(col(51), col(51), 1.0, LN3_EPS,
                                          A.mult, A.add))
        k.drain(V)
        rsqrt_col_lnexp(51, 52, 53)
        k.op(V, lambda e: e.tensor_tensor(col(54), col(48), col(52), A.mult))
        k.drain(V)
        k.op(V, lambda e: e.tensor_tensor(col(55), col(42), col(44), A.subtract))
        k.op(V, lambda e: e.tensor_tensor(col(56), col(44), col(43), A.subtract))
        k.drain(V)
        k.op(V, lambda e: e.tensor_tensor(col(55), col(55), col(56), A.max))
        k.drain(V)
        k.op(V, lambda e: e.tensor_tensor(col(55), col(55), col(54), A.mult))
        k.drain(V)
        k.op(V, lambda e, qt=qt: e.tensor_scalar(ccol[:, qt:qt + 1], col(55),
                                                 1.0 / Q_B, None, A.mult))
        k.op(V, lambda e: e.reciprocal(col(57), col(55)))
        k.drain(V)
        k.op(V, lambda e: e.tensor_scalar(col(57), col(57), Q_B, None, A.mult))
        k.drain(V)
        k.op(V, lambda e: e.tensor_tensor(col(58), col(54), col(57), A.mult))
        k.drain(V)
        k.op(V, lambda e: e.tensor_scalar(col(59), col(44), -1.0, col(58),
                                          A.mult, A.mult))
        k.drain(V)
        k.op(V, lambda e: e.tensor_scalar(FMIX[:, o_x2:o_x2 + E],
                                          FMIX[:, o_u:o_u + E],
                                          col(58), col(59), A.mult, A.add))
        k.drain(V)
        vv = k.op(V, lambda e: e.tensor_scalar(
            MIX[:, m_xqt:m_xqt + E], FMIX[:, o_x2:o_x2 + E], QCLIP, -QCLIP,
            A.min, A.max), inc="ev")
        k.wait(Y, "ev", vv)
        blocked_store(Y, xq_scr, S, MIX, m_xqt, qt, EC)

    for g in range(ST // QG):
        qr_group(g)
        qr_ev = k.ctr["ev"]
        for qq in range(QG):
            attn_tile(g * QG + qq, qr_ev)
            run_prep(33)
    run_prep_until(FC_PREP_END)
    k.barrier()

    if phases < 8:
        k.barrier()
        k.finalize()
        return nc

    for ec in range(EC):
        k.dmat(Y, BUF_A[:, ec * S:(ec + 1) * S], xq_scr[ec, :, :], inc="hf")
    k.wait(V, "hf", k.ctr["hf"])
    k.wait(T, "hf", k.ctr["hf"])
    vv = k.op(V, lambda e: e.tensor_scalar(gscol[:, 0:ST], ccol[:, 0:ST],
                                           gam2[:, 0:1], None, A.mult), inc="ev")
    k.drain(V)
    k.wait(SC, "ev", vv)

    fc_sv = [0, 0, 0, 0]
    fc_vv = [0, 0, 0, 0]
    fc_st = [0, 0, 0, 0]
    fcb_tv = [0, 0]
    for fs in range(FS):
        half = (fs % 2) * EC * 512
        k.wait(Y, "st", k.ctr["st"])
        if fs >= 2:
            k.wait(Y, "et", fcb_tv[fs % 2])
        for ec in range(EC):
            wl = k.dma(Y, BUF_B[:, half + ec * 512:half + (ec + 1) * 512],
                       ap(tfc_scr, ec * (F * 128) + fs * (512 * 128),
                          [[512, 128], [1, 512]]),
                       inc="fc%d" % (fs % 2))
        k.wait(T, "fc%d" % (fs % 2), wl)
        for qt in range(ST):
            si = fs * ST + qt
            sq = si % 4
            def mm_fc(e, qt=qt, half=half, sq=sq):
                for ec in range(EC):
                    last = e.matmul(p_sc[:, sq * 512:(sq + 1) * 512],
                             BUF_A[:, ec * S + qt * 128:ec * S + (qt + 1) * 128],
                             BUF_B[:, half + ec * 512:half + (ec + 1) * 512],
                             start=(ec == 0), stop=(ec == EC - 1))
                return last
            if fc_sv[sq]:
                k.wait(T, "es", fc_sv[sq])
            tv = k.op(T, mm_fc, inc="et")
            k.wait(SC, "et", tv)
            ho = m_h2a + sq * 512
            if fc_vv[sq]:
                k.wait(SC, "ev", fc_vv[sq])
                k.wait(SC, "hs%d" % sq, fc_st[sq])
            sv = k.op(SC, lambda e, qt=qt, ho=ho, sq=sq, si=si: e.activation(
                MIX[:, ho:ho + 512], p_sc[:, sq * 512:(sq + 1) * 512], AF.Gelu,
                scale=gscol[:, qt:qt + 1], accum_out=FMIX[:, o_ssum + si:o_ssum + si + 1]),
                inc="es")
            fc_sv[sq] = sv
            k.wait(Y, "es", sv)
            fc_st[sq] = k.dma(
                Y, h2_scr[qt * 128:(qt + 1) * 128, fs * 512:(fs + 1) * 512],
                MIX[:, ho:ho + 512], inc="hs%d" % sq)
            k.wait(V, "es", sv)
            k.op(V, lambda e, ho=ho, si=si: e.scalar_tensor_tensor(
                FMIX[:, o_x2:o_x2 + 512], MIX[:, ho:ho + 512], 1.0,
                MIX[:, ho:ho + 512],
                A.mult, A.mult, accum_out=FMIX[:, o_ssq + si:o_ssq + si + 1]))
            k.op(V, lambda e, ho=ho, si=si: e.reduce_max(
                FMIX[:, o_smx + si:o_smx + si + 1], MIX[:, ho:ho + 512], axis=AX.X))
            k.op(V, lambda e, ho=ho, si=si: e.tensor_reduce(
                FMIX[:, o_smn + si:o_smn + si + 1], MIX[:, ho:ho + 512], axis=AX.X,
                op=A.min))
            k.drain(V)
            vv = k.op(V, lambda e: e.drain(), inc="ev")
            fc_vv[sq] = vv
            if qt % 4 == 3:
                run_prep(8)
        fcb_tv[fs % 2] = tv
    run_prep(10 ** 9)
    k.barrier()

    if phases < 9:
        k.barrier()
        k.finalize()
        return nc

    NQ = min(4, (E * S) // (128 * F))
    assert NQ >= 1 and ST % NQ == 0
    xh_par = [0, 0]
    xh_n = [0]

    def d_tile(qt):
        FW = 9472
        k.op(V, lambda e, qt=qt: e.reduce_sum(
            col(40), ap(FMIX, o_ssum + qt, [[FW, 128], [ST, FS]]), axis=AX.X))
        k.op(V, lambda e, qt=qt: e.reduce_sum(
            col(41), ap(FMIX, o_ssq + qt, [[FW, 128], [ST, FS]]), axis=AX.X))
        k.op(V, lambda e, qt=qt: e.reduce_max(
            col(42), ap(FMIX, o_smx + qt, [[FW, 128], [ST, FS]]), axis=AX.X))
        k.op(V, lambda e, qt=qt: e.tensor_reduce(
            col(43), ap(FMIX, o_smn + qt, [[FW, 128], [ST, FS]]), axis=AX.X,
            op=A.min))
        k.drain(V)
        k.op(V, lambda e: e.tensor_scalar(col(44), col(40), 1.0 / F, None, A.mult))
        k.drain(V)
        k.op(V, lambda e: e.scalar_tensor_tensor(col(45), col(44), 1.0, col(44),
                                                 A.mult, A.mult))
        k.op(V, lambda e: e.tensor_scalar(col(46), col(41), 1.0 / F, None, A.mult))
        k.drain(V)
        k.op(V, lambda e: e.tensor_tensor(col(46), col(46), col(45), A.subtract))
        k.drain(V)
        k.op(V, lambda e: e.tensor_scalar(col(47), col(46), 1.0, LN3_EPS,
                                          A.mult, A.add))
        k.drain(V)
        rsqrt_col(47, 48, 49)
        k.op(V, lambda e: e.tensor_tensor(col(55), col(42), col(44), A.subtract))
        k.op(V, lambda e: e.tensor_tensor(col(56), col(44), col(43), A.subtract))
        k.drain(V)
        k.op(V, lambda e: e.tensor_tensor(col(55), col(55), col(56), A.max))
        k.drain(V)
        k.op(V, lambda e: e.tensor_tensor(col(55), col(55), col(48), A.mult))
        k.drain(V)
        k.op(V, lambda e, qt=qt: e.tensor_scalar(cpcol[:, qt:qt + 1], col(55),
                                                 1.0 / Q_B, None, A.mult))
        k.op(V, lambda e: e.reciprocal(col(57), col(55)))
        k.drain(V)
        k.op(V, lambda e, qt=qt: e.tensor_scalar(gpcol[:, qt:qt + 1],
                                                 cpcol[:, qt:qt + 1],
                                                 gam2[:, 1:2], None, A.mult))
        k.op(V, lambda e: e.tensor_scalar(col(57), col(57), Q_B, None, A.mult))
        k.drain(V)
        k.op(V, lambda e: e.tensor_tensor(col(58), col(48), col(57), A.mult))
        k.drain(V)
        vv = k.op(V, lambda e: e.tensor_scalar(col(59), col(44), -1.0, col(58),
                                               A.mult, A.mult), inc="ev")
        k.wait(SC, "ev", vv)
        k.wait(Y, "ev", vv)
        DQW = 6144 if EC * S >= 12288 else 0
        for q in range(F // 2048):
            n = xh_n[0]
            par = (n % 2) if DQW else 0
            qoff = par * DQW
            if xh_par[par]:
                k.wait(Y, "xh%d" % par, xh_par[par])
            lv = k.dma(Y, BUF_B[:, qoff:qoff + 2048],
                       h2_scr[qt * 128:(qt + 1) * 128, q * 2048:(q + 1) * 2048],
                       inc="xl")
            k.wait(SC, "xl", lv)
            sv = k.op(SC, lambda e, qoff=qoff: e.activation(
                BUF_B[:, qoff + 2048:qoff + 4096], BUF_B[:, qoff:qoff + 2048],
                AF.Identity, bias=col(59), scale=col(58)), inc="es")
            k.wait(V, "es", sv)
            vv = k.op(V, lambda e, qoff=qoff: e.tensor_scalar(
                BUF_B[:, qoff + 4096:qoff + 6144],
                BUF_B[:, qoff + 2048:qoff + 4096],
                QCLIP, -QCLIP, A.min, A.max), inc="ev")
            k.wait(Y, "ev", vv)
            dims = [[128, 128], [S * 128, 2048 // 128], [1, 128]]
            xh_par[par] = k.dma(
                Y, ap(xh_scr, (q * (2048 // 128)) * (S * 128) + qt * 128 * 128,
                      dims), BUF_B[:, qoff + 4096:qoff + 6144], inc="xh%d" % par)
            xh_n[0] = n + 1

    k.wait(Y, "st", k.ctr["st"])
    y_last = [0]
    ws_lv = [0] * 8
    ws_tv = [0] * 8
    hf_lv = {}
    hq_lv = [0] * 8
    for qg in range(ST // NQ):
        for j in range(NQ):
            d_tile(qg * NQ + j)
        k.wait(Y, "xh0", k.ctr.get("xh0", 0))
        k.wait(Y, "xh1", k.ctr.get("xh1", 0))
        if qg > 0:
            k.wait(Y, "et", k.ctr["et"])

        def issue_hf(fc, qg=qg):
            b = fc % 8
            if hq_lv[b]:
                k.wait(Y, "hq%d" % b, hq_lv[b])
            hq_lv[b] = k.dmat(
                Y, BUF_A[:, fc * (NQ * 128):(fc + 1) * (NQ * 128)],
                xh_scr[fc, qg * NQ * 128:(qg + 1) * NQ * 128, :],
                inc="hq%d" % b)
            hf_lv[fc] = hq_lv[b]
        for fc in range(min(8, FC)):
            issue_hf(fc)
        for es in range(ESL):
            x2_lv = {}
            k.wait(Y, "ev", k.ctr["ev"])
            for j in range(NQ):
                qt = qg * NQ + j
                x2_lv[j] = k.dma(Y, FMIX[:, o_x + 1024 + j * 512:
                                         o_x + 1536 + j * 512],
                                 x2_scr[qt * 128:(qt + 1) * 128,
                                        es * 512:(es + 1) * 512], inc="xl")
            def issue_slab(nf, es=es):
                bb = nf % 8
                if ws_tv[bb]:
                    k.wait(Y, "et", ws_tv[bb])
                ws_lv[bb] = k.dma(
                    Y, WS[:, bb * 512:bb * 512 + 512],
                    ap(tpj_scr, nf * (E * 128) + es * (512 * 128),
                       [[512, 128], [1, 512]]), inc="pq%d" % bb)
            for nf in range(4):
                issue_slab(nf)
            for fc in range(FC):
                b = fc % 8
                if fc + 4 < FC:
                    issue_slab(fc + 4)
                if es == 0:
                    if fc + 8 < FC:
                        issue_hf(fc + 8)
                    k.wait(T, "hq%d" % (fc % 8), hf_lv.pop(fc))
                k.wait(T, "pq%d" % b, ws_lv[b])
                def mm_pj(e, es=es, fc=fc, b=b):
                    for j in range(NQ):
                        last = e.matmul(p_sc[:, j * 512:(j + 1) * 512],
                                 BUF_A[:, fc * (NQ * 128) + j * 128:
                                       fc * (NQ * 128) + (j + 1) * 128],
                                 WS[:, b * 512:b * 512 + 512],
                                 start=(fc == 0), stop=(fc == FC - 1))
                    return last
                ws_tv[b] = k.op(T, mm_pj, inc="et")
            k.wait(V, "et", ws_tv[(FC - 1) % 8])
            if y_last[0]:
                k.wait(V, "st", y_last[0])
            k.wait(V, "xl", x2_lv[NQ - 1])
            for j in range(NQ):
                qt = qg * NQ + j
                vv = k.op(V, lambda e, j=j, qt=qt: e.scalar_tensor_tensor(
                    FMIX[:, o_u + j * 512:o_u + (j + 1) * 512],
                    p_sc[:, j * 512:(j + 1) * 512],
                    gpcol[:, qt:qt + 1],
                    FMIX[:, o_x + 1024 + j * 512:o_x + 1536 + j * 512],
                    A.mult, A.add), inc="ev")
                k.wait(Y, "ev", vv)
                y_last[0] = k.dma(Y, y_out[qt * 128:(qt + 1) * 128,
                                           es * 512:(es + 1) * 512],
                                  FMIX[:, o_u + j * 512:o_u + (j + 1) * 512],
                                  inc="st")
            k.wait(T, "ev", k.ctr["ev"])

    k.barrier()
    k.finalize()
    return nc


def rope_tables(cfg, jnp, jax):
    D = 128
    freqs = 1.0 / 10000.0 ** (jnp.arange(0, D, 2, dtype=jnp.float32) / D)
    ang = jnp.arange(cfg.S, dtype=jnp.float32)[:, None] * freqs[None, :]
    cos = np.asarray(jnp.cos(ang))
    sin = np.asarray(jnp.sin(ang))
    c2 = np.empty((128, cfg.S), np.float32)
    s2 = np.empty((128, cfg.S), np.float32)
    c2[0::2, :] = cos.T
    c2[1::2, :] = cos.T
    s2[0::2, :] = -sin.T
    s2[1::2, :] = sin.T
    return c2, s2


_CACHED = {}


def _tables(cfg):
    import jax
    import jax.numpy as jnp
    cpu = jax.devices("cpu")[0]
    D = 128
    with jax.default_device(cpu):
        freqs = 1.0 / 10000.0 ** (jnp.arange(0, D, 2, dtype=jnp.float32) / D)
        ang = jnp.arange(cfg.S, dtype=jnp.float32)[:, None] * freqs[None, :]
        cos = np.asarray(jnp.cos(ang))
        sin = np.asarray(jnp.sin(ang))
    c2 = np.empty((128, cfg.S), np.float32)
    s2 = np.empty((128, cfg.S), np.float32)
    c2[0::2, :] = cos.T
    c2[1::2, :] = cos.T
    s2[0::2, :] = -sin.T
    s2[1::2, :] = sin.T
    return c2, s2


def kernel(L0_attn_h, L0_attn_w_d_kv, L0_attn_w_u_k, L0_attn_w_u_v,
           L0_attn_w_d_q, L0_attn_w_u_q, L0_attn_w_kr, L0_attn_w_qr,
           L0_attn_w_o, L0_attn_g1, L0_attn_g2, L0_attn_g3,
           L0_ffn_w_fc, L0_ffn_w_proj):
    import ml_dtypes
    from concourse.bass_utils import run_bass_kernel_spmd
    cfg = Cfg(S=2048, E=2048, C=512, F=8192)
    if "nc" not in _CACHED:
        _CACHED["nc"] = build_nc(cfg)
    nc = _CACHED["nc"]
    c2, s2 = _tables(cfg)
    x = np.ascontiguousarray(np.asarray(L0_attn_h, dtype=np.float32))
    B = x.shape[0]
    shared = {
        "wdq": np.ascontiguousarray(np.asarray(L0_attn_w_d_q, np.float32)),
        "wdkv": np.ascontiguousarray(np.asarray(L0_attn_w_d_kv, np.float32)),
        "wuk": np.ascontiguousarray(np.asarray(L0_attn_w_u_k, np.float32)),
        "wuv": np.ascontiguousarray(np.asarray(L0_attn_w_u_v, np.float32)),
        "wuq": np.ascontiguousarray(np.asarray(L0_attn_w_u_q, np.float32)),
        "wkr": np.ascontiguousarray(np.asarray(L0_attn_w_kr, np.float32)),
        "wqr": np.ascontiguousarray(np.asarray(L0_attn_w_qr, np.float32)),
        "wo": np.ascontiguousarray(np.asarray(L0_attn_w_o, np.float32)),
        "wfc": np.ascontiguousarray(np.asarray(L0_ffn_w_fc, np.float32)),
        "wproj": np.ascontiguousarray(np.asarray(L0_ffn_w_proj, np.float32)),
        "g1b": np.ascontiguousarray(np.broadcast_to(
            np.asarray(L0_attn_g1, np.float32)[None, :], (128, cfg.C))),
        "g2b": np.ascontiguousarray(np.broadcast_to(
            np.asarray(L0_attn_g2, np.float32)[None, :], (128, cfg.C))),
        "g3b": np.ascontiguousarray(np.broadcast_to(
            np.asarray(L0_attn_g3, np.float32)[None, :], (128, cfg.E))),
        "c2t": c2, "s2t": s2,
        "identw": np.eye(128, dtype=np.float32).astype(ml_dtypes.bfloat16),
        "ones_r": np.ones((1, 128), np.float32),
    }
    in_maps = [dict(shared, x=x[b]) for b in range(B)]
    res = run_bass_kernel_spmd(nc, in_maps, core_ids=list(range(B)))
    out = np.stack([np.asarray(res.results[b]["y"], np.float32)
                    for b in range(B)], axis=0)
    return out
